# version 01
# baseline (speedup 1.0000x reference)
import numpy as np
import ml_dtypes
from contextlib import ExitStack

import concourse.bass as bass
import concourse.mybir as mybir
from concourse.bass_utils import run_bass_kernel_spmd

F32 = mybir.dt.float32
BF16 = mybir.dt.bfloat16
AF = mybir.ActivationFunctionType
ALU = mybir.AluOpType
AX = mybir.AxisListType
BF = ml_dtypes.bfloat16

NC = 8
D = 1024
NCH = 8
TPC = 2048
TOK = 16384
SEQ = 8192
TB = 512
NTB = TPC // TB


def _bf(a):
    return np.ascontiguousarray(a.astype(BF))


def _f32(a):
    return np.ascontiguousarray(a.astype(np.float32))


def _chunkvec(v):
    return np.ascontiguousarray(v.reshape(NCH, 128).T.astype(np.float32))


class _Chain:
    def __init__(self, sem):
        self.sem = sem
        self.n = 0

    def step(self, ins):
        self.n += 1
        ins.then_inc(self.sem, 1)

    def sync(self, eng):
        eng.wait_ge(self.sem, self.n)

    def milestone(self, eng, sem, k=1):
        self.sync(eng)
        eng.nop().then_inc(sem, k)


def build_ln0():
    nc = bass.Bass("TRN2", target_bir_lowering=False)
    xt = nc.dram_tensor("xt", [D, TPC], F32, kind="ExternalInput").ap()
    ones128 = nc.dram_tensor("ones128", [128, 128], F32, kind="ExternalInput").ap()
    xn_out = nc.dram_tensor("xn_out", [D, TPC], BF16, kind="ExternalOutput").ap()

    with ExitStack() as st:
        E = st.enter_context
        xs = E(nc.sbuf_tensor("xs", [128, NCH * TPC], F32))
        sqb = E(nc.sbuf_tensor("sqb", [128, NCH * TB], F32))
        xnb = E(nc.sbuf_tensor("xnb", [128, NCH * TPC], BF16))
        sb_ones = E(nc.sbuf_tensor("sb_ones", [128, 128], F32))
        mp = E(nc.sbuf_tensor("mp", [128, TB], F32))
        vp = E(nc.sbuf_tensor("vp", [128, TB], F32))
        sp = E(nc.sbuf_tensor("sp", [128, TB], F32))
        rp = E(nc.sbuf_tensor("rp", [128, TB], F32))
        tmp = E(nc.sbuf_tensor("tmp", [128, TB], F32))
        psS = E(nc.psum_tensor("psS", [128, TB], F32))
        psQ = E(nc.psum_tensor("psQ", [128, TB], F32))
        s_ld = E(nc.semaphore("s_ld"))
        s_sq = E(nc.semaphore("s_sq"))
        s_st1 = E(nc.semaphore("s_st1"))
        s_row1 = E(nc.semaphore("s_row1"))
        s_sqrt = E(nc.semaphore("s_sqrt"))
        s_app = E(nc.semaphore("s_app"))
        s_out = E(nc.semaphore("s_out"))
        s_vch = E(nc.semaphore("s_vch"))
        block = E(nc.Block())

        def cts(c, tb):
            return slice(c * TPC + tb * TB, c * TPC + (tb + 1) * TB)

        @block.sync
        def _(sy):
            sy.dma_start(out=xs[:].rearrange("p (c t) -> p c t", c=NCH),
                         in_=xt.rearrange("(c p) t -> p c t", p=128)).then_inc(s_ld, 16)
            sy.dma_start(out=sb_ones[:], in_=ones128[:]).then_inc(s_ld, 16)
            sy.wait_ge(s_app, NTB)
            sy.dma_start(out=xn_out.rearrange("(c p) t -> p c t", p=128),
                         in_=xnb[:].rearrange("p (c t) -> p c t", c=NCH)).then_inc(s_out, 16)
            sy.wait_ge(s_out, 16)

        @block.scalar
        def _(sc):
            sc.wait_ge(s_ld, 32)
            for tb in range(NTB):
                if tb > 0:
                    sc.wait_ge(s_st1, tb)
                for c in range(NCH):
                    ins = sc.activation(sqb[:, c * TB:(c + 1) * TB], xs[:, cts(c, tb)], AF.Square)
                ins.then_inc(s_sq, 1)
                sc.wait_ge(s_row1, tb + 1)
                sc.activation(sp[:], vp[:], AF.Sqrt).then_inc(s_sqrt, 1)

        @block.tensor
        def _(te):
            for tb in range(NTB):
                te.wait_ge(s_sq, tb + 1)
                if tb > 0:
                    te.wait_ge(s_row1, tb)
                for c in range(NCH):
                    te.matmul(psS[:], sb_ones[:], xs[:, cts(c, tb)], start=(c == 0), stop=(c == NCH - 1))
                for c in range(NCH):
                    ins = te.matmul(psQ[:], sb_ones[:], sqb[:, c * TB:(c + 1) * TB], start=(c == 0), stop=(c == NCH - 1))
                ins.then_inc(s_st1, 1)

        @block.vector
        def _(ve):
            ch = _Chain(s_vch)
            for tb in range(NTB):
                ve.wait_ge(s_st1, tb + 1)
                ch.sync(ve)
                ch.step(ve.tensor_scalar_mul(mp[:], psS[:], 1.0 / D))
                ch.step(ve.tensor_scalar(vp[:], psQ[:], 1.0 / D, 1e-5, ALU.mult, ALU.add))
                ch.sync(ve)
                ch.step(ve.tensor_mul(tmp[:], mp[:], mp[:]))
                ch.sync(ve)
                ch.step(ve.tensor_sub(vp[:], vp[:], tmp[:]))
                ch.milestone(ve, s_row1)
                ve.wait_ge(s_sqrt, tb + 1)
                ch.step(ve.reciprocal(rp[:], sp[:]))
                ch.sync(ve)
                for c in range(NCH):
                    ch.sync(ve)
                    ch.step(ve.tensor_sub(tmp[:], xs[:, cts(c, tb)], mp[:]))
                    ch.sync(ve)
                    ch.step(ve.tensor_mul(xnb[:, cts(c, tb)], tmp[:], rp[:]))
                ch.milestone(ve, s_app)

    return nc


def prep_ln0(x_full):
    xt_full = np.ascontiguousarray(x_full.reshape(TOK, D).T)
    ones128 = np.ones((128, 128), np.float32)
    maps = []
    for i in range(NC):
        maps.append({
            "xt": np.ascontiguousarray(xt_full[:, i * TPC:(i + 1) * TPC]),
            "ones128": ones128,
        })
    return maps



R128 = 128
NFFT = 16384


def dft_consts():
    p = np.arange(R128)
    C = np.cos(2 * np.pi * np.outer(p, p) / R128).astype(np.float32)
    S = np.sin(2 * np.pi * np.outer(p, p) / R128).astype(np.float32)
    TWC = np.cos(2 * np.pi * np.outer(p, p) / NFFT).astype(np.float32)
    TWS = np.sin(2 * np.pi * np.outer(p, p) / NFFT).astype(np.float32)
    return {
        "Cb": _bf(C), "Sb": _bf(S), "Snb": _bf(-S),
        "R1a": _bf(np.concatenate([C, -S], axis=1)),
        "R1b": _bf(np.concatenate([S, C], axis=1)),
        "R2a": _bf(np.concatenate([C, S], axis=1)),
        "R2b": _bf(np.concatenate([-S, C], axis=1)),
        "TWC4": _f32(np.tile(TWC, (1, 4))),
        "TWS4": _f32(np.tile(TWS, (1, 4))),
    }


def build_conv():
    nc = bass.Bass("TRN2", target_bir_lowering=False)
    xnr = nc.dram_tensor("xnr", [128, TOK], BF16, kind="ExternalInput").ap()
    tt = nc.dram_tensor("tt", [128, 4096], F32, kind="ExternalInput").ap()
    cns = {}
    for nm, shp, dt in [("Cb", [128, 128], BF16), ("Sb", [128, 128], BF16), ("Snb", [128, 128], BF16),
                        ("R1a", [128, 256], BF16), ("R1b", [128, 256], BF16),
                        ("R2a", [128, 256], BF16), ("R2b", [128, 256], BF16),
                        ("TWC4", [128, 512], F32), ("TWS4", [128, 512], F32)]:
        cns[nm] = nc.dram_tensor(nm, shp, dt, kind="ExternalInput").ap()
    s_out = nc.dram_tensor("s_out", [128, TOK], F32, kind="ExternalOutput").ap()

    NB = 32
    BCH = 4

    with ExitStack() as st:
        E = st.enter_context
        ub = E(nc.sbuf_tensor("ub", [64, 128 * 256], BF16))
        tb_ = E(nc.sbuf_tensor("tb_", [32, 128 * 128], BF16))
        That = E(nc.sbuf_tensor("That", [128, 128 * 256], BF16))
        sb = {nm: E(nc.sbuf_tensor("sb_" + nm, [128, 128], BF16)) for nm in ["Cb", "Sb", "Snb"]}
        for nm in ["R1a", "R1b", "R2a", "R2b"]:
            sb[nm] = E(nc.sbuf_tensor("sb_" + nm, [128, 256], BF16))
        for nm in ["TWC4", "TWS4"]:
            sb[nm] = E(nc.sbuf_tensor("sb_" + nm, [128, 512], F32))
        TRt = E(nc.sbuf_tensor("TRt", [128, BCH * 128], BF16))
        TIt = E(nc.sbuf_tensor("TIt", [128, BCH * 128], BF16))
        YRt = E(nc.sbuf_tensor("YRt", [128, BCH * 128], BF16))
        YIt = E(nc.sbuf_tensor("YIt", [128, BCH * 128], BF16))
        HRt = E(nc.sbuf_tensor("HRt", [128, BCH * 128], BF16))
        HIt = E(nc.sbuf_tensor("HIt", [128, BCH * 128], BF16))
        t1 = E(nc.sbuf_tensor("t1", [128, 512], F32))
        t2 = E(nc.sbuf_tensor("t2", [128, 512], F32))
        ybuf = E(nc.sbuf_tensor("ybuf", [64, 2 * BCH * 256], F32))
        psA = E(nc.psum_tensor("psA", [128, BCH * 256], F32))
        psX = E(nc.psum_tensor("psX", [128, BCH * 256], F32))
        psG = E(nc.psum_tensor("psG", [128, BCH * 256], F32))
        psY = E(nc.psum_tensor("psY", [128, BCH * 256], F32))
        s_ld = E(nc.semaphore("s_ld"))
        s_ldg = E(nc.semaphore("s_ldg"))
        s_tte = E(nc.semaphore("s_tte"))
        s_ttw = E(nc.semaphore("s_ttw"))
        s_tta = E(nc.semaphore("s_tta"))
        s_te = E(nc.semaphore("s_te"))
        s_ve = E(nc.semaphore("s_ve"))
        s_st = E(nc.semaphore("s_st"))
        s_vch = E(nc.semaphore("s_vch"))
        block = E(nc.Block())

        NLD = 10

        def ch_sl(buf, c, w):
            return buf[:, c * w:(c + 1) * w]

        @block.sync
        def _(sy):
            sy.dma_start(out=ub[:].rearrange("p (c h f) -> p c h f", c=128, h=2),
                         in_=xnr.rearrange("c (h p f) -> p c h f", h=2, p=64)).then_inc(s_ld, 16)
            for nm in ["Cb", "Sb", "Snb", "R1a", "R1b", "R2a", "R2b", "TWC4", "TWS4"]:
                sy.dma_start(out=sb[nm][:], in_=cns[nm]).then_inc(s_ld, 16)
            for bi in range(NB):
                sy.wait_ge(s_ve, 4 * bi + 4)
                par = bi % 2
                sy.dma_start(
                    out=s_out[bi * BCH:(bi + 1) * BCH, :].rearrange("c (h p f) -> p c h f", h=2, p=64),
                    in_=ybuf[:, par * BCH * 256:(par + 1) * BCH * 256].rearrange("p (c h f) -> p c h f", c=BCH, h=2),
                ).then_inc(s_st, 16)
                sy.wait_ge(s_st, 16 * (bi + 1))

        @block.tensor
        def _(te):
            te.wait_ge(s_ld, 16 * NLD)
            te.wait_ge(s_ldg, 16)
            for bi in range(NB):
                if bi > 0:
                    te.wait_ge(s_ttw, bi)
                for c in range(BCH):
                    ch = bi * BCH + c
                    ins = te.matmul(psA[:, c * 256:(c + 1) * 256], tb_[:, ch * 128:(ch + 1) * 128],
                                    sb["R1a"][0:32, :], start=True, stop=True)
                ins.then_inc(s_tte, 1)
                te.wait_ge(s_ttw, bi + 1)
                if bi > 0:
                    te.wait_ge(s_tta, bi)
                for c in range(BCH):
                    te.matmul(psX[:, c * 256:c * 256 + 128], sb["Cb"][:], ch_sl(TRt, c, 128),
                              start=True, stop=False, skip_group_check=True)
                    te.matmul(psX[:, c * 256:c * 256 + 128], sb["Sb"][:], ch_sl(TIt, c, 128),
                              start=False, stop=True, skip_group_check=True)
                    te.matmul(psX[:, c * 256 + 128:(c + 1) * 256], sb["Cb"][:], ch_sl(TIt, c, 128),
                              start=True, stop=False, skip_group_check=True)
                    ins = te.matmul(psX[:, c * 256 + 128:(c + 1) * 256], sb["Snb"][:], ch_sl(TRt, c, 128),
                                    start=False, stop=True, skip_group_check=True)
                ins.then_inc(s_tte, 1)
            for bi in range(NB):
                if bi == 0:
                    te.wait_ge(s_ttw, NB)
                else:
                    te.wait_ge(s_ve, 4 * bi - 3)
                for c in range(BCH):
                    ch = bi * BCH + c
                    te.matmul(psA[:, c * 256:(c + 1) * 256], ub[:, ch * 256:ch * 256 + 128],
                              sb["R1a"][0:64, :], start=True, stop=False)
                    ins = te.matmul(psA[:, c * 256:(c + 1) * 256], ub[:, ch * 256 + 128:(ch + 1) * 256],
                                    sb["R1b"][0:64, :], start=False, stop=True)
                ins.then_inc(s_te, 1)
                te.wait_ge(s_ve, 4 * bi + 1)
                if bi == 0:
                    te.wait_ge(s_tta, NB)
                for c in range(BCH):
                    te.matmul(psX[:, c * 256:c * 256 + 128], sb["Cb"][:], ch_sl(TRt, c, 128),
                              start=True, stop=False, skip_group_check=True)
                    te.matmul(psX[:, c * 256:c * 256 + 128], sb["Sb"][:], ch_sl(TIt, c, 128),
                              start=False, stop=True, skip_group_check=True)
                    te.matmul(psX[:, c * 256 + 128:(c + 1) * 256], sb["Cb"][:], ch_sl(TIt, c, 128),
                              start=True, stop=False, skip_group_check=True)
                    ins = te.matmul(psX[:, c * 256 + 128:(c + 1) * 256], sb["Snb"][:], ch_sl(TRt, c, 128),
                                    start=False, stop=True, skip_group_check=True)
                ins.then_inc(s_te, 1)
                te.wait_ge(s_ve, 4 * bi + 2)
                for c in range(BCH):
                    te.matmul(psG[:, c * 256:(c + 1) * 256], ch_sl(YRt, c, 128), sb["R2a"][:],
                              start=True, stop=False)
                    ins = te.matmul(psG[:, c * 256:(c + 1) * 256], ch_sl(YIt, c, 128), sb["R2b"][:],
                                    start=False, stop=True)
                ins.then_inc(s_te, 1)
                te.wait_ge(s_ve, 4 * bi + 3)
                for c in range(BCH):
                    te.matmul(psY[:, c * 256:c * 256 + 128], sb["Cb"][:], ch_sl(HRt, c, 128),
                              start=True, stop=False, skip_group_check=True)
                    te.matmul(psY[:, c * 256:c * 256 + 128], sb["Snb"][:], ch_sl(HIt, c, 128),
                              start=False, stop=True, skip_group_check=True)
                    te.matmul(psY[:, c * 256 + 128:(c + 1) * 256], sb["Cb"][:], ch_sl(HIt, c, 128),
                              start=True, stop=False, skip_group_check=True)
                    ins = te.matmul(psY[:, c * 256 + 128:(c + 1) * 256], sb["Sb"][:], ch_sl(HRt, c, 128),
                                    start=False, stop=True, skip_group_check=True)
                ins.then_inc(s_te, 1)

        def twiddle(ve, ch, src_ps, conj, outR, outI):
            Ar = src_ps[:].rearrange("p (c two f) -> p c two f", c=BCH, two=2)[:, :, 0, :]
            Ai = src_ps[:].rearrange("p (c two f) -> p c two f", c=BCH, two=2)[:, :, 1, :]
            wc = sb["TWC4"][:].rearrange("p (c f) -> p c f", c=BCH)
            ws = sb["TWS4"][:].rearrange("p (c f) -> p c f", c=BCH)
            v3 = lambda t: t[:].rearrange("p (c f) -> p c f", c=BCH)
            ch.sync(ve)
            ch.step(ve.tensor_tensor(v3(t1), Ar, wc, ALU.mult))
            ch.step(ve.tensor_tensor(v3(t2), Ai, ws, ALU.mult))
            ch.sync(ve)
            if conj:
                ch.step(ve.tensor_tensor(v3(outR), v3(t1), v3(t2), ALU.subtract))
            else:
                ch.step(ve.tensor_tensor(v3(outR), v3(t1), v3(t2), ALU.add))
            ch.sync(ve)
            ch.step(ve.tensor_tensor(v3(t1), Ai, wc, ALU.mult))
            ch.step(ve.tensor_tensor(v3(t2), Ar, ws, ALU.mult))
            ch.sync(ve)
            if conj:
                ch.step(ve.tensor_tensor(v3(outI), v3(t1), v3(t2), ALU.add))
            else:
                ch.step(ve.tensor_tensor(v3(outI), v3(t1), v3(t2), ALU.subtract))

        @block.vector
        def _(ve):
            ch = _Chain(s_vch)
            for bi in range(NB):
                ve.wait_ge(s_tte, 2 * bi + 1)
                twiddle(ve, ch, psA, False, TRt, TIt)
                ch.milestone(ve, s_ttw)
            for bi in range(NB):
                ve.wait_ge(s_te, 4 * bi + 1)
                twiddle(ve, ch, psA, False, TRt, TIt)
                ch.milestone(ve, s_ve)
                ve.wait_ge(s_te, 4 * bi + 2)
                if bi == 0:
                    ve.wait_ge(s_tta, NB)
                Xr = psX[:].rearrange("p (c two f) -> p c two f", c=BCH, two=2)[:, :, 0, :]
                Xi = psX[:].rearrange("p (c two f) -> p c two f", c=BCH, two=2)[:, :, 1, :]
                tsl = That[:, bi * BCH * 256:(bi + 1) * BCH * 256].rearrange(
                    "p (c two f) -> p c two f", c=BCH, two=2)
                Tr = tsl[:, :, 0, :]
                Ti = tsl[:, :, 1, :]
                v3 = lambda t: t[:].rearrange("p (c f) -> p c f", c=BCH)
                ch.sync(ve)
                ch.step(ve.tensor_tensor(v3(t1), Xr, Tr, ALU.mult))
                ch.step(ve.tensor_tensor(v3(t2), Xi, Ti, ALU.mult))
                ch.sync(ve)
                ch.step(ve.tensor_tensor(v3(YRt), v3(t1), v3(t2), ALU.subtract))
                ch.sync(ve)
                ch.step(ve.tensor_tensor(v3(t1), Xr, Ti, ALU.mult))
                ch.step(ve.tensor_tensor(v3(t2), Xi, Tr, ALU.mult))
                ch.sync(ve)
                ch.step(ve.tensor_tensor(v3(YIt), v3(t1), v3(t2), ALU.add))
                ch.milestone(ve, s_ve)
                ve.wait_ge(s_te, 4 * bi + 3)
                twiddle(ve, ch, psG, True, HRt, HIt)
                ch.milestone(ve, s_ve)
                ve.wait_ge(s_te, 4 * bi + 4)
                if bi >= 2:
                    ve.wait_ge(s_st, 16 * (bi - 1))
                par = bi % 2
                ch.sync(ve)
                ch.step(ve.tensor_scalar_mul(
                    ybuf[:, par * BCH * 256:(par + 1) * BCH * 256],
                    psY[0:64, :], 1.0 / NFFT))
                ch.milestone(ve, s_ve)

        @block.gpsimd
        def _(gp):
            gp.dma_start(out=tb_[:].rearrange("p (c f) -> p c f", c=128),
                         in_=tt.rearrange("c (p f) -> p c f", p=32)).then_inc(s_ldg, 16)

        @block.scalar
        def _(sc):
            for bi in range(NB):
                sc.wait_ge(s_tte, 2 * bi + 2)
                sc.activation(That[:, bi * BCH * 256:(bi + 1) * BCH * 256], psX[:],
                              AF.Identity).then_inc(s_tta, 1)

    return nc


def prep_conv(xnT_full, t_full, core):
    cn = dft_consts()
    m = dict(cn)
    m["xnr"] = np.ascontiguousarray(xnT_full[core * 128:(core + 1) * 128, :])
    m["tt"] = np.ascontiguousarray(t_full[:, core * 128:(core + 1) * 128].T.astype(np.float32))
    return m



TBF = 256
NBF = TPC // TBF


def build_ffn0():
    nc = bass.Bass("TRN2", target_bir_lowering=False)
    xnc_d = nc.dram_tensor("xnc", [D, TPC], BF16, kind="ExternalInput").ap()
    ssc_d = nc.dram_tensor("ssc", [D, TPC], F32, kind="ExternalInput").ap()
    xtc_d = nc.dram_tensor("xtc", [D, TPC], F32, kind="ExternalInput").ap()
    w1t_d = nc.dram_tensor("w1t", [D, 4096], F32, kind="ExternalInput").ap()
    w2t_d = nc.dram_tensor("w2t", [4096, D], F32, kind="ExternalInput").ap()
    rwt_d = nc.dram_tensor("rwt", [D, 512], F32, kind="ExternalInput").ap()
    ut_d = nc.dram_tensor("ut", [D, 512], F32, kind="ExternalInput").ap()
    un_d = nc.dram_tensor("un", [512, D], F32, kind="ExternalInput").ap()
    vn_d = nc.dram_tensor("vn", [512, D], F32, kind="ExternalInput").ap()
    gam_d = nc.dram_tensor("gam", [128, D], F32, kind="ExternalInput").ap()
    lng_d = nc.dram_tensor("lng", [128, NCH], F32, kind="ExternalInput").ap()
    lnb_d = nc.dram_tensor("lnbv", [128, NCH], F32, kind="ExternalInput").ap()
    rbt_d = nc.dram_tensor("rbt", [128, 512], F32, kind="ExternalInput").ap()
    onb_d = nc.dram_tensor("onb", [128, 128], BF16, kind="ExternalInput").ap()
    idf_d = nc.dram_tensor("idf", [128, 128], F32, kind="ExternalInput").ap()
    cst_d = nc.dram_tensor("cst", [128, 4], F32, kind="ExternalInput").ap()
    x1_out = nc.dram_tensor("x1_out", [D, TPC], F32, kind="ExternalOutput").ap()

    with ExitStack() as st:
        E = st.enter_context
        xnc = E(nc.sbuf_tensor("b_xnc", [128, NCH * TPC], BF16))
        w1t = E(nc.sbuf_tensor("b_w1t", [128, NCH * 4096], BF16))
        w2s = E(nc.sbuf_tensor("b_w2s", [128, 2 * 32 * 128], BF16))
        rwt = E(nc.sbuf_tensor("b_rwt", [128, NCH * 512], BF16))
        ut = E(nc.sbuf_tensor("b_ut", [128, NCH * 512], BF16))
        vg = E(nc.sbuf_tensor("b_vg", [128, 4 * D], BF16))
        uv2 = E(nc.sbuf_tensor("b_uv2", [128, 24], F32))
        gamt = E(nc.sbuf_tensor("b_gamt", [128, D], BF16))
        lng = E(nc.sbuf_tensor("b_lng", [128, NCH], F32))
        lnbv = E(nc.sbuf_tensor("b_lnbv", [128, NCH], F32))
        rbt = E(nc.sbuf_tensor("b_rbt", [128, 512], F32))
        onb = E(nc.sbuf_tensor("b_onb", [128, 128], BF16))
        idf = E(nc.sbuf_tensor("b_idf", [128, 128], F32))
        cst = E(nc.sbuf_tensor("b_cst", [128, 4], F32))
        lnb = E(nc.sbuf_tensor("b_lnb", [128, NCH * TBF], BF16))
        sqb = E(nc.sbuf_tensor("b_sqb", [128, NCH * TBF], BF16))
        mp = E(nc.sbuf_tensor("b_mp", [128, TBF], F32))
        vp = E(nc.sbuf_tensor("b_vp", [128, TBF], F32))
        sp = E(nc.sbuf_tensor("b_sp", [128, TBF], F32))
        rp = E(nc.sbuf_tensor("b_rp", [128, TBF], F32))
        tq = E(nc.sbuf_tensor("b_tq", [128, TBF], F32))
        atm = E(nc.sbuf_tensor("b_atm", [128, 512], F32))
        rcl = E(nc.sbuf_tensor("b_rcl", [128, 512], F32))
        mtm = E(nc.sbuf_tensor("b_mtm", [128, 512], F32))
        eab = E(nc.sbuf_tensor("b_eab", [128, 512], F32))
        lab = E(nc.sbuf_tensor("b_lab", [128, 512], F32))
        top8 = E(nc.sbuf_tensor("b_top8", [128, 8], F32))
        srow = E(nc.sbuf_tensor("b_srow", [128, 4], F32))
        am2 = E(nc.sbuf_tensor("b_am2", [128, 512], F32))
        amf = E(nc.sbuf_tensor("b_amf", [128, 4 * TBF], BF16))
        wfm = E(nc.sbuf_tensor("b_wfm", [128, 4 * TBF], BF16))
        gh = E(nc.sbuf_tensor("b_gh", [128, 32 * TBF], BF16))
        g0f = E(nc.sbuf_tensor("b_g0f", [128, 2 * TBF], F32))
        ssb = E(nc.sbuf_tensor("b_ssb", [128, NCH * TBF], F32))
        xsb = E(nc.sbuf_tensor("b_xsb", [128, NCH * TBF], F32))
        x1t = E(nc.sbuf_tensor("b_x1t", [128, 2 * TBF], F32))
        psS = E(nc.psum_tensor("psS", [128, 512], F32))
        psQ = E(nc.psum_tensor("psQ", [128, 512], F32))
        psR = E(nc.psum_tensor("psR", [128, 512], F32))
        psT = E(nc.psum_tensor("psT", [128, 512], F32))
        psHL = E(nc.psum_tensor("psHL", [128, 512], F32))
        psH0 = E(nc.psum_tensor("psH0", [128, 512], F32))
        psH1 = E(nc.psum_tensor("psH1", [128, 512], F32))
        psO = E(nc.psum_tensor("psO", [128, 512], F32))
        sems = {}
        for nm in ["s_ld", "s_wg", "s_nrm", "s_sq", "s_st1", "s_pl", "s_sqr", "s_lnB",
                   "s_amm", "s_apre", "s_achn", "s_tanh", "s_am2", "s_tr", "s_trev",
                   "s_hmm", "s_wfm", "s_s1", "s_gel", "s_omm", "s_sil", "s_epi",
                   "s_w2", "s_ldb", "s_x1", "s_vch", "s_ach", "s_topk", "s_vg", "s_nsq"]:
            sems[nm] = E(nc.semaphore(nm))
        S = sems
        block = E(nc.Block())

        psH = [psH0, psH1]

        def xn_sl(c, tb, w=TBF):
            return xnc[:, c * TPC + tb * w:c * TPC + (tb + 1) * w]

        @block.sync
        def _(sy):
            sy.dma_start(out=xnc[:].rearrange("p (c t) -> p c t", c=NCH),
                         in_=xnc_d.rearrange("(c p) t -> p c t", p=128)).then_inc(S["s_ld"], 16)
            sy.dma_start(out=lng[:], in_=lng_d[:]).then_inc(S["s_ld"], 16)
            sy.dma_start(out=lnbv[:], in_=lnb_d[:]).then_inc(S["s_ld"], 16)
            sy.dma_start(out=rbt[:], in_=rbt_d[:]).then_inc(S["s_ld"], 16)
            sy.dma_start(out=onb[:], in_=onb_d[:]).then_inc(S["s_ld"], 16)
            sy.dma_start(out=idf[:], in_=idf_d[:]).then_inc(S["s_ld"], 16)
            sy.dma_start(out=cst[:], in_=cst_d[:]).then_inc(S["s_ld"], 16)
            sy.wait_ge(S["s_ld"], 16 * 7)
            for j in range(8):
                src_t = un_d if j < 4 else vn_d
                jc = j % 4
                if j > 0:
                    sy.wait_ge(S["s_nrm"], j)
                sy.dma_start(out=ssb[:, 0:D], in_=src_t[jc * 128:(jc + 1) * 128, :]).then_inc(S["s_ld"], 16)
            sy.wait_ge(S["s_nrm"], 8)
            for tb in range(NBF):
                if tb > 0:
                    sy.wait_ge(S["s_epi"], 8 * tb)
                sy.dma_start(out=ssb[:].rearrange("p (c t) -> p c t", c=NCH),
                             in_=ssc_d.rearrange("(c p) t -> p c t", p=128)[:, :, tb * TBF:(tb + 1) * TBF]).then_inc(S["s_ldb"], 16)
                sy.dma_start(out=xsb[:].rearrange("p (c t) -> p c t", c=NCH),
                             in_=xtc_d.rearrange("(c p) t -> p c t", p=128)[:, :, tb * TBF:(tb + 1) * TBF]).then_inc(S["s_ldb"], 16)
                sy.wait_ge(S["s_ldb"], 32 * (tb + 1))
                for dc in range(NCH):
                    g = tb * NCH + dc
                    sy.wait_ge(S["s_epi"], g + 1)
                    sy.dma_start(out=x1_out[dc * 128:(dc + 1) * 128, tb * TBF:(tb + 1) * TBF],
                                 in_=x1t[:, (g % 2) * TBF:(g % 2 + 1) * TBF]).then_inc(S["s_x1"], 16)
                    sy.wait_ge(S["s_x1"], 16 * (g + 1))

        @block.gpsimd
        def _(gp):
            gp.dma_start(out=w1t[:].rearrange("p (c h) -> p c h", c=NCH),
                         in_=w1t_d.rearrange("(c p) h -> p c h", p=128)).then_inc(S["s_wg"], 16)
            gp.dma_start(out=rwt[:].rearrange("p (c j) -> p c j", c=NCH),
                         in_=rwt_d.rearrange("(c p) j -> p c j", p=128)).then_inc(S["s_wg"], 16)
            gp.dma_start(out=ut[:].rearrange("p (c j) -> p c j", c=NCH),
                         in_=ut_d.rearrange("(c p) j -> p c j", p=128)).then_inc(S["s_wg"], 16)
            gp.dma_start(out=gamt[:], in_=gam_d[:]).then_inc(S["s_wg"], 16)
            for jc in range(4):
                gp.dma_start(out=vg[:, jc * D:(jc + 1) * D],
                             in_=vn_d[jc * 128:(jc + 1) * 128, :]).then_inc(S["s_wg"], 16)
            for tb in range(NBF):
                for dc in range(NCH):
                    g = tb * NCH + dc
                    if g >= 2:
                        gp.wait_ge(S["s_omm"], g - 1)
                    if g > 0:
                        gp.wait_ge(S["s_w2"], 16 * g)
                    par = g % 2
                    gp.dma_start(out=w2s[:, par * 32 * 128:(par + 1) * 32 * 128].rearrange("p (h f) -> p h f", h=32),
                                 in_=w2t_d[:, dc * 128:(dc + 1) * 128].rearrange("(h p) f -> p h f", p=128)
                                 ).then_inc(S["s_w2"], 16)

        @block.tensor
        def _(te):
            te.wait_ge(S["s_ld"], 16 * 7)
            te.wait_ge(S["s_wg"], 16 * 8)
            te.wait_ge(S["s_vg"], 1)
            for tb in range(NBF):
                te.wait_ge(S["s_sq"], tb + 1)
                if tb > 0:
                    te.wait_ge(S["s_pl"], tb)
                for c in range(NCH):
                    te.matmul(psS[:, 0:TBF], onb[:], xn_sl(c, tb), start=(c == 0), stop=(c == NCH - 1))
                for c in range(NCH):
                    ins = te.matmul(psQ[:, 0:TBF], onb[:], sqb[:, c * TBF:(c + 1) * TBF],
                                    start=(c == 0), stop=(c == NCH - 1))
                ins.then_inc(S["s_st1"], 1)
                for u in range(2):
                    uu = tb * 2 + u
                    te.wait_ge(S["s_lnB"], tb + 1)
                    te.wait_ge(S["s_apre"], uu)
                    for c in range(NCH):
                        ins = te.matmul(psR[:], lnb[:, c * TBF + u * 128:c * TBF + (u + 1) * 128],
                                        rwt[:, c * 512:(c + 1) * 512], start=(c == 0), stop=(c == NCH - 1))
                    ins.then_inc(S["s_amm"], 1)
                    te.wait_ge(S["s_am2"], uu + 1)
                    for jc in range(4):
                        ti = uu * 4 + jc
                        te.wait_ge(S["s_trev"], ti)
                        te.transpose(psT[:, 0:128], am2[:, jc * 128:(jc + 1) * 128], idf[:]).then_inc(S["s_tr"], 1)
                for jc in range(4):
                    hj = tb * 4 + jc
                    te.wait_ge(S["s_wfm"], hj)
                    for c in range(NCH):
                        ins = te.matmul(psHL[:, 0:TBF], ut[:, c * 512 + jc * 128:c * 512 + (jc + 1) * 128],
                                        xn_sl(c, tb), start=(c == 0), stop=(c == NCH - 1))
                    ins.then_inc(S["s_hmm"], 1)
                for hc in range(32):
                    hh = tb * 32 + hc
                    if hh >= 2:
                        te.wait_ge(S["s_gel"], hh - 1)
                    pb = psH[hc % 2]
                    for c in range(NCH):
                        ins = te.matmul(pb[:, 0:TBF], w1t[:, c * 4096 + hc * 128:c * 4096 + (hc + 1) * 128],
                                        xn_sl(c, tb), start=(c == 0), stop=(c == NCH - 1))
                    ins.then_inc(S["s_s1"], 1)
                te.wait_ge(S["s_gel"], 32 * (tb + 1))
                te.wait_ge(S["s_wfm"], 4 * (tb + 1))
                for dc in range(NCH):
                    g = tb * NCH + dc
                    te.wait_ge(S["s_sil"], g)
                    te.wait_ge(S["s_w2"], 16 * (g + 1))
                    par = g % 2
                    for hc in range(32):
                        te.matmul(psO[:, 0:TBF], w2s[:, par * 32 * 128 + hc * 128:par * 32 * 128 + (hc + 1) * 128],
                                  gh[:, hc * TBF:(hc + 1) * TBF], start=(hc == 0), stop=False)
                    for jc in range(4):
                        ins = te.matmul(psO[:, 0:TBF], vg[:, jc * D + dc * 128:jc * D + (dc + 1) * 128],
                                        wfm[:, jc * TBF:(jc + 1) * TBF], start=False, stop=(jc == 3))
                    ins.then_inc(S["s_omm"], 1)

        @block.scalar
        def _(sc):
            ach = _Chain(S["s_ach"])
            sc.wait_ge(S["s_ld"], 16 * 7)
            sc.wait_ge(S["s_nrm"], 8)
            sc.activation(uv2[:, 8:16], uv2[:, 0:8], AF.Sqrt).then_inc(S["s_nsq"], 1)
            for tb in range(NBF):
                if tb > 0:
                    sc.wait_ge(S["s_st1"], tb)
                for c in range(NCH):
                    ins = sc.activation(sqb[:, c * TBF:(c + 1) * TBF], xn_sl(c, tb), AF.Square)
                ins.then_inc(S["s_sq"], 1)
                sc.wait_ge(S["s_pl"], tb + 1)
                sc.activation(sp[:], vp[:], AF.Sqrt).then_inc(S["s_sqr"], 1)
                for u in range(2):
                    uu = tb * 2 + u
                    sc.wait_ge(S["s_apre"], uu + 1)
                    ach.step(sc.activation(mtm[:], rcl[:], AF.Abs))
                    ach.sync(sc)
                    ach.step(sc.activation(eab[:], mtm[:], AF.Exp, scale=cst[:, 0:1]))
                    ach.sync(sc)
                    ach.step(sc.activation(lab[:], eab[:], AF.Ln, bias=cst[:, 1:2]))
                    ach.milestone(sc, S["s_achn"])
                    sc.wait_ge(S["s_topk"], uu + 1)
                    ach.step(sc.activation(srow[:, 2:3], srow[:, 0:1], AF.Tanh))
                    ach.milestone(sc, S["s_tanh"])
                for hc in range(32):
                    hh = tb * 32 + hc
                    sc.wait_ge(S["s_s1"], hh + 1)
                    sc.activation(gh[:, hc * TBF:(hc + 1) * TBF], psH[hc % 2][:, 0:TBF],
                                  AF.Gelu).then_inc(S["s_gel"], 1)
                for dc in range(NCH):
                    g = tb * NCH + dc
                    sc.wait_ge(S["s_omm"], g + 1)
                    if g >= 2:
                        sc.wait_ge(S["s_epi"], g - 1)
                    sc.activation(g0f[:, (g % 2) * TBF:(g % 2 + 1) * TBF], psO[:, 0:TBF],
                                  AF.Silu).then_inc(S["s_sil"], 1)

        @block.vector
        def _(ve):
            ch = _Chain(S["s_vch"])
            ve.wait_ge(S["s_wg"], 16 * 8)
            for j in range(8):
                ve.wait_ge(S["s_ld"], 16 * (8 + j))
                ch.sync(ve)
                ch.step(ve.tensor_mul(xsb[:, 0:D], ssb[:, 0:D], ssb[:, 0:D]))
                ch.sync(ve)
                ch.step(ve.reduce_sum(uv2[:, j:j + 1], xsb[:, 0:D], axis=AX.X))
                ch.milestone(ve, S["s_nrm"])
            ve.wait_ge(S["s_nsq"], 1)
            ch.step(ve.reciprocal(uv2[:, 8:16], uv2[:, 8:16]))
            ch.sync(ve)
            ch.step(ve.tensor_mul(uv2[:, 16:20], uv2[:, 8:12], uv2[:, 12:16]))
            for jc in range(4):
                ch.sync(ve)
                ch.step(ve.tensor_mul(vg[:, jc * D:(jc + 1) * D], vg[:, jc * D:(jc + 1) * D], gamt[:]))
            ch.milestone(ve, S["s_vg"])
            for tb in range(NBF):
                ve.wait_ge(S["s_st1"], tb + 1)
                ch.sync(ve)
                ch.step(ve.tensor_scalar_mul(mp[:], psS[:, 0:TBF], 1.0 / D))
                ch.step(ve.tensor_scalar(vp[:], psQ[:, 0:TBF], 1.0 / D, 1e-5, ALU.mult, ALU.add))
                ch.sync(ve)
                ch.step(ve.tensor_mul(tq[:], mp[:], mp[:]))
                ch.sync(ve)
                ch.step(ve.tensor_sub(vp[:], vp[:], tq[:]))
                ch.milestone(ve, S["s_pl"])
                ve.wait_ge(S["s_sqr"], tb + 1)
                ch.step(ve.reciprocal(rp[:], sp[:]))
                ch.sync(ve)
                if tb > 0:
                    ve.wait_ge(S["s_amm"], 2 * tb)
                for c in range(NCH):
                    ch.sync(ve)
                    ch.step(ve.tensor_sub(tq[:], xn_sl(c, tb), mp[:]))
                    ch.sync(ve)
                    ch.step(ve.tensor_mul(tq[:], tq[:], rp[:]))
                    ch.sync(ve)
                    ch.step(ve.tensor_scalar(lnb[:, c * TBF:(c + 1) * TBF], tq[:], lng[:, c:c + 1],
                                             lnbv[:, c:c + 1], ALU.mult, ALU.add))
                ch.milestone(ve, S["s_lnB"])
                for u in range(2):
                    uu = tb * 2 + u
                    ve.wait_ge(S["s_amm"], uu + 1)
                    ch.sync(ve)
                    ch.step(ve.tensor_add(rcl[:], psR[:], rbt[:]))
                    ch.sync(ve)
                    ch.step(ve.tensor_scalar(rcl[:], rcl[:], 10.0, -10.0, ALU.min, ALU.max))
                    ch.milestone(ve, S["s_apre"])
                    ch.step(ve.tensor_scalar_max(atm[:], rcl[:], 0.0))
                    ve.wait_ge(S["s_achn"], uu + 1)
                    ch.sync(ve)
                    ch.step(ve.tensor_add(atm[:], atm[:], lab[:]))
                    ch.sync(ve)
                    ch.step(ve.max(top8[:], atm[:]))
                    ch.sync(ve)
                    ch.step(ve.reduce_sum(srow[:, 0:1], top8[:], axis=AX.X))
                    ch.milestone(ve, S["s_topk"])
                    ch.step(ve.tensor_single_scalar(mtm[:], atm[:], top8[:, 7:8], ALU.is_ge))
                    ch.step(ve.tensor_scalar_add(srow[:, 1:2], srow[:, 0:1], 1e-6))
                    ch.sync(ve)
                    ch.step(ve.tensor_mul(am2[:], atm[:], mtm[:]))
                    ch.step(ve.reciprocal(srow[:, 1:2], srow[:, 1:2]))
                    ve.wait_ge(S["s_tanh"], uu + 1)
                    ch.sync(ve)
                    ch.step(ve.tensor_mul(srow[:, 3:4], srow[:, 2:3], srow[:, 1:2]))
                    ch.sync(ve)
                    ch.step(ve.tensor_scalar_mul(am2[:], am2[:], srow[:, 3:4]))
                    ch.milestone(ve, S["s_am2"])
                    for jc in range(4):
                        ti = uu * 4 + jc
                        ve.wait_ge(S["s_tr"], ti + 1)
                        ch.sync(ve)
                        ch.step(ve.tensor_copy(amf[:, jc * TBF + u * 128:jc * TBF + (u + 1) * 128],
                                               psT[:, 0:128]))
                        ch.milestone(ve, S["s_trev"])
                for jc in range(4):
                    hj = tb * 4 + jc
                    ve.wait_ge(S["s_hmm"], hj + 1)
                    ch.sync(ve)
                    ch.step(ve.scalar_tensor_tensor(wfm[:, jc * TBF:(jc + 1) * TBF], psHL[:, 0:TBF],
                                                    uv2[:, 16 + jc:17 + jc], amf[:, jc * TBF:(jc + 1) * TBF],
                                                    ALU.mult, ALU.mult))
                    ch.milestone(ve, S["s_wfm"])
                ve.wait_ge(S["s_ldb"], 32 * (tb + 1))
                for dc in range(NCH):
                    g = tb * NCH + dc
                    ve.wait_ge(S["s_sil"], g + 1)
                    if g >= 2:
                        ve.wait_ge(S["s_x1"], 16 * (g - 1))
                    ch.sync(ve)
                    ch.step(ve.tensor_mul(tq[:], ssb[:, dc * TBF:(dc + 1) * TBF],
                                          g0f[:, (g % 2) * TBF:(g % 2 + 1) * TBF]))
                    ch.sync(ve)
                    ch.step(ve.tensor_add(x1t[:, (g % 2) * TBF:(g % 2 + 1) * TBF],
                                          xsb[:, dc * TBF:(dc + 1) * TBF], tq[:]))
                    ch.milestone(ve, S["s_epi"])

    return nc


def prep_ffn0(xnT_full, sT_full, xT_full, ins, core):
    i0, i1 = core * TPC, (core + 1) * TPC
    m = {
        "xnc": np.ascontiguousarray(xnT_full[:, i0:i1]),
        "ssc": np.ascontiguousarray(sT_full[:, i0:i1].astype(np.float32)),
        "xtc": np.ascontiguousarray(xT_full[:, i0:i1]),
        "w1t": np.ascontiguousarray(ins["L0_ffn_W1"].T.astype(np.float32)),
        "w2t": np.ascontiguousarray(ins["L0_ffn_W2"].T.astype(np.float32)),
        "rwt": np.ascontiguousarray(ins["L0_ffn_router_W"].T.astype(np.float32)),
        "ut": np.ascontiguousarray(ins["L0_ffn_raw_U"].T.astype(np.float32)),
        "un": _f32(ins["L0_ffn_raw_U"]),
        "vn": _f32(ins["L0_ffn_raw_V"]),
        "gam": np.ascontiguousarray(np.tile(ins["L0_ffn_gamma"].reshape(1, D), (128, 1)).astype(np.float32)),
        "lng": _chunkvec(np.asarray(ins["L0_ffn_ln_g"])),
        "lnbv": _chunkvec(np.asarray(ins["L0_ffn_ln_b"])),
        "rbt": np.ascontiguousarray(np.tile(ins["L0_ffn_router_b"].reshape(1, 512), (128, 1)).astype(np.float32)),
        "onb": _bf(np.ones((128, 128))),
        "idf": _f32(np.eye(128)),
        "cst": np.ascontiguousarray(np.tile(np.array([[-1.0, 1.0, 0.0, 0.0]], np.float32), (128, 1))),
    }
    return m



def rope_planes():
    inv = 1.0 / 10000.0 ** (np.arange(0, 64, 2, dtype=np.float64)[:8] / 64.0)
    h = np.arange(16, dtype=np.float64)
    ang = np.outer(h, inv)
    ce = np.tile(np.cos(ang).reshape(1, 128), (128, 1)).astype(np.float32)
    se = np.tile(np.sin(ang).reshape(1, 128), (128, 1)).astype(np.float32)
    return ce, se


def build_qkv():
    nc = bass.Bass("TRN2", target_bir_lowering=False)
    x1c_d = nc.dram_tensor("x1c", [D, TPC], F32, kind="ExternalInput").ap()
    wqt_d = nc.dram_tensor("wqt", [D, D], F32, kind="ExternalInput").ap()
    wkat_d = nc.dram_tensor("wkat", [D, 256], F32, kind="ExternalInput").ap()
    wkbt_d = nc.dram_tensor("wkbt", [256, D], F32, kind="ExternalInput").ap()
    wvat_d = nc.dram_tensor("wvat", [D, 256], F32, kind="ExternalInput").ap()
    wvbt_d = nc.dram_tensor("wvbt", [256, D], F32, kind="ExternalInput").ap()
    onf_d = nc.dram_tensor("onf", [128, 128], F32, kind="ExternalInput").ap()
    ce_d = nc.dram_tensor("cep", [128, 128], F32, kind="ExternalInput").ap()
    se_d = nc.dram_tensor("sep", [128, 128], F32, kind="ExternalInput").ap()
    xn1_out = nc.dram_tensor("xn1_out", [D, TPC], BF16, kind="ExternalOutput").ap()
    q_out = nc.dram_tensor("q_out", [D, TPC], BF16, kind="ExternalOutput").ap()
    k_out = nc.dram_tensor("k_out", [D, TPC], BF16, kind="ExternalOutput").ap()
    v_out = nc.dram_tensor("v_out", [TPC, D], BF16, kind="ExternalOutput").ap()

    with ExitStack() as st:
        E = st.enter_context
        xs = E(nc.sbuf_tensor("d_xs", [128, NCH * TPC], F32))
        xn1b = E(nc.sbuf_tensor("d_xn1b", [128, NCH * TPC], BF16))
        sqb = E(nc.sbuf_tensor("d_sqb", [128, NCH * TB], F32))
        wqt = E(nc.sbuf_tensor("d_wqt", [128, NCH * D], BF16))
        wkat = E(nc.sbuf_tensor("d_wkat", [128, NCH * 256], BF16))
        wkbt = E(nc.sbuf_tensor("d_wkbt", [128, 2 * D], BF16))
        wvat = E(nc.sbuf_tensor("d_wvat", [128, NCH * 256], BF16))
        wvbt = E(nc.sbuf_tensor("d_wvbt", [128, 2 * D], BF16))
        onf = E(nc.sbuf_tensor("d_onf", [128, 128], F32))
        cep = E(nc.sbuf_tensor("d_cep", [128, 128], F32))
        sep = E(nc.sbuf_tensor("d_sep", [128, 128], F32))
        latk = E(nc.sbuf_tensor("d_latk", [128, 2 * TPC], BF16))
        latv = E(nc.sbuf_tensor("d_latv", [128, 2 * TPC], BF16))
        mp = E(nc.sbuf_tensor("d_mp", [128, TB], F32))
        vp = E(nc.sbuf_tensor("d_vp", [128, TB], F32))
        sp = E(nc.sbuf_tensor("d_sp", [128, TB], F32))
        rp = E(nc.sbuf_tensor("d_rp", [128, TB], F32))
        tq = E(nc.sbuf_tensor("d_tq", [128, TB], F32))
        t1 = E(nc.sbuf_tensor("d_t1", [128, 128], F32))
        t2 = E(nc.sbuf_tensor("d_t2", [128, 128], F32))
        rE = E(nc.sbuf_tensor("d_rE", [128, 128], F32))
        rO = E(nc.sbuf_tensor("d_rO", [128, 128], F32))
        qkb = E(nc.sbuf_tensor("d_qkb", [128, 2 * TB], BF16))
        vtb = E(nc.sbuf_tensor("d_vtb", [128, 2 * D], BF16))
        psS = E(nc.psum_tensor("psS", [128, 512], F32))
        psQ = E(nc.psum_tensor("psQ", [128, 512], F32))
        psL = E(nc.psum_tensor("psL", [128, 512], F32))
        psK = E(nc.psum_tensor("psK", [128, 512], F32))
        psV = E(nc.psum_tensor("psV", [128, 1024], F32))
        sems = {}
        for nm in ["s_ld", "s_wg", "s_rot", "s_sq", "s_st1", "s_pl", "s_sqr", "s_app",
                   "s_lat", "s_late", "s_qk", "s_qke", "s_qkd", "s_vmm", "s_vme", "s_vd",
                   "s_xo", "s_vch", "s_ach"]:
            sems[nm] = E(nc.semaphore(nm))
        S = sems
        block = E(nc.Block())

        def cts(c, tb, w=TB):
            return slice(c * TPC + tb * w, c * TPC + (tb + 1) * w)

        @block.sync
        def _(sy):
            sy.dma_start(out=xs[:].rearrange("p (c t) -> p c t", c=NCH),
                         in_=x1c_d.rearrange("(c p) t -> p c t", p=128)).then_inc(S["s_ld"], 16)
            sy.dma_start(out=onf[:], in_=onf_d[:]).then_inc(S["s_ld"], 16)
            sy.dma_start(out=cep[:], in_=ce_d[:]).then_inc(S["s_ld"], 16)
            sy.dma_start(out=sep[:], in_=se_d[:]).then_inc(S["s_ld"], 16)
            sy.wait_ge(S["s_app"], NTB)
            sy.dma_start(out=xn1_out.rearrange("(c p) t -> p c t", p=128),
                         in_=xn1b[:].rearrange("p (c t) -> p c t", c=NCH)).then_inc(S["s_xo"], 16)
            for g in range(64):
                tb, dco = (g % 32) // 8, g % 8
                dst = q_out if g < 32 else k_out
                sy.wait_ge(S["s_qke"], g + 1)
                sy.dma_start(out=dst[dco * 128:(dco + 1) * 128, tb * TB:(tb + 1) * TB],
                             in_=qkb[:, (g % 2) * TB:(g % 2 + 1) * TB]).then_inc(S["s_qkd"], 16)
                sy.wait_ge(S["s_qkd"], 16 * (g + 1))
            for tc in range(16):
                sy.wait_ge(S["s_vme"], tc + 1)
                sy.dma_start(out=v_out[tc * 128:(tc + 1) * 128, :],
                             in_=vtb[:, (tc % 2) * D:(tc % 2 + 1) * D]).then_inc(S["s_vd"], 16)
                sy.wait_ge(S["s_vd"], 16 * (tc + 1))
            sy.wait_ge(S["s_xo"], 16)

        @block.gpsimd
        def _(gp):
            gp.dma_start(out=wqt[:].rearrange("p (c o) -> p c o", c=NCH),
                         in_=wqt_d.rearrange("(c p) o -> p c o", p=128)).then_inc(S["s_wg"], 16)
            gp.dma_start(out=wkat[:].rearrange("p (c o) -> p c o", c=NCH),
                         in_=wkat_d.rearrange("(c p) o -> p c o", p=128)).then_inc(S["s_wg"], 16)
            gp.dma_start(out=wkbt[:].rearrange("p (c o) -> p c o", c=2),
                         in_=wkbt_d.rearrange("(c p) o -> p c o", p=128)).then_inc(S["s_wg"], 16)
            gp.dma_start(out=wvat[:].rearrange("p (c o) -> p c o", c=NCH),
                         in_=wvat_d.rearrange("(c p) o -> p c o", p=128)).then_inc(S["s_wg"], 16)
            gp.dma_start(out=wvbt[:].rearrange("p (c o) -> p c o", c=2),
                         in_=wvbt_d.rearrange("(c p) o -> p c o", p=128)).then_inc(S["s_wg"], 16)

        @block.tensor
        def _(te):
            te.wait_ge(S["s_ld"], 64)
            for tb in range(NTB):
                te.wait_ge(S["s_sq"], tb + 1)
                if tb > 0:
                    te.wait_ge(S["s_pl"], tb)
                for c in range(NCH):
                    te.matmul(psS[:], onf[:], xs[:, cts(c, tb)], start=(c == 0), stop=(c == NCH - 1))
                for c in range(NCH):
                    ins = te.matmul(psQ[:], onf[:], sqb[:, c * TB:(c + 1) * TB],
                                    start=(c == 0), stop=(c == NCH - 1))
                ins.then_inc(S["s_st1"], 1)
            te.wait_ge(S["s_rot"], 1)
            te.wait_ge(S["s_app"], NTB)
            gl = 0
            for tb in range(NTB):
                for which in range(2):
                    wat = wkat if which == 0 else wvat
                    lat = latk if which == 0 else latv
                    for lc in range(2):
                        gl += 1
                        te.wait_ge(S["s_late"], gl - 1)
                        for c in range(NCH):
                            ins = te.matmul(psL[:], wat[:, c * 256 + lc * 128:c * 256 + (lc + 1) * 128],
                                            xn1b[:, cts(c, tb)], start=(c == 0), stop=(c == NCH - 1))
                        ins.then_inc(S["s_lat"], 1)
            for g in range(64):
                tb, dco = (g % 32) // 8, g % 8
                te.wait_ge(S["s_qke"], g)
                if g < 32:
                    for c in range(NCH):
                        ins = te.matmul(psK[:], wqt[:, c * D + dco * 128:c * D + (dco + 1) * 128],
                                        xn1b[:, cts(c, tb)], start=(c == 0), stop=(c == NCH - 1))
                else:
                    for lc in range(2):
                        ins = te.matmul(psK[:], wkbt[:, lc * D + dco * 128:lc * D + (dco + 1) * 128],
                                        latk[:, lc * TPC + tb * TB:lc * TPC + (tb + 1) * TB],
                                        start=(lc == 0), stop=(lc == 1))
                ins.then_inc(S["s_qk"], 1)
            for tc in range(16):
                te.wait_ge(S["s_vme"], tc)
                for hf in range(2):
                    for lc in range(2):
                        ins = te.matmul(psV[:, hf * 512:(hf + 1) * 512],
                                        latv[:, lc * TPC + tc * 128:lc * TPC + (tc + 1) * 128],
                                        wvbt[:, lc * D + hf * 512:lc * D + (hf + 1) * 512],
                                        start=(lc == 0), stop=(lc == 1))
                ins.then_inc(S["s_vmm"], 1)

        @block.scalar
        def _(sc):
            sc.wait_ge(S["s_ld"], 64)
            for tb in range(NTB):
                if tb > 0:
                    sc.wait_ge(S["s_st1"], tb)
                for c in range(NCH):
                    ins = sc.activation(sqb[:, c * TB:(c + 1) * TB], xs[:, cts(c, tb)], AF.Square)
                ins.then_inc(S["s_sq"], 1)
                sc.wait_ge(S["s_pl"], tb + 1)
                sc.activation(sp[:], vp[:], AF.Sqrt).then_inc(S["s_sqr"], 1)
            for gl in range(1, 17):
                sc.wait_ge(S["s_lat"], gl)
                tb = (gl - 1) // 4
                which = ((gl - 1) % 4) // 2
                lc = (gl - 1) % 2
                lat = latk if which == 0 else latv
                sc.activation(lat[:, lc * TPC + tb * TB:lc * TPC + (tb + 1) * TB], psL[:],
                              AF.Identity).then_inc(S["s_late"], 1)
            for g in range(64):
                sc.wait_ge(S["s_qk"], g + 1)
                if g >= 2:
                    sc.wait_ge(S["s_qkd"], 16 * (g - 1))
                sc.activation(qkb[:, (g % 2) * TB:(g % 2 + 1) * TB], psK[:],
                              AF.Identity).then_inc(S["s_qke"], 1)
            for tc in range(16):
                sc.wait_ge(S["s_vmm"], tc + 1)
                if tc >= 2:
                    sc.wait_ge(S["s_vd"], 16 * (tc - 1))
                sc.activation(vtb[:, (tc % 2) * D:(tc % 2 + 1) * D], psV[:],
                              AF.Identity).then_inc(S["s_vme"], 1)

        @block.vector
        def _(ve):
            ch = _Chain(S["s_vch"])
            ve.wait_ge(S["s_wg"], 16 * 5)
            ve.wait_ge(S["s_ld"], 64)
            for mat, nch_ in ((wqt, NCH), (wkbt, 2)):
                for c in range(nch_):
                    w4 = mat[:, c * D:(c + 1) * D].rearrange("p (h a two) -> p h a two", h=16, a=32)
                    ev = w4[:, :, 0:8, 0:1]
                    od = w4[:, :, 0:8, 1:2]
                    c3 = cep[:].rearrange("p (h f o) -> p h f o", h=16, o=1)
                    s3 = sep[:].rearrange("p (h f o) -> p h f o", h=16, o=1)
                    v3 = lambda t: t[:].rearrange("p (h f o) -> p h f o", h=16, o=1)
                    ch.sync(ve)
                    ch.step(ve.tensor_tensor(v3(t1), ev, c3, ALU.mult))
                    ch.step(ve.tensor_tensor(v3(t2), od, s3, ALU.mult))
                    ch.sync(ve)
                    ch.step(ve.tensor_tensor(v3(rE), v3(t1), v3(t2), ALU.subtract))
                    ch.sync(ve)
                    ch.step(ve.tensor_tensor(v3(t1), ev, s3, ALU.mult))
                    ch.step(ve.tensor_tensor(v3(t2), od, c3, ALU.mult))
                    ch.sync(ve)
                    ch.step(ve.tensor_tensor(v3(rO), v3(t1), v3(t2), ALU.add))
                    ch.sync(ve)
                    ch.step(ve.tensor_copy(ev, v3(rE)))
                    ch.step(ve.tensor_copy(od, v3(rO)))
            ch.milestone(ve, S["s_rot"])
            for tb in range(NTB):
                ve.wait_ge(S["s_st1"], tb + 1)
                ch.sync(ve)
                ch.step(ve.tensor_scalar_mul(mp[:], psS[:], 1.0 / D))
                ch.step(ve.tensor_scalar(vp[:], psQ[:], 1.0 / D, 1e-5, ALU.mult, ALU.add))
                ch.sync(ve)
                ch.step(ve.tensor_mul(tq[:], mp[:], mp[:]))
                ch.sync(ve)
                ch.step(ve.tensor_sub(vp[:], vp[:], tq[:]))
                ch.milestone(ve, S["s_pl"])
                ve.wait_ge(S["s_sqr"], tb + 1)
                ch.step(ve.reciprocal(rp[:], sp[:]))
                ch.sync(ve)
                for c in range(NCH):
                    ch.sync(ve)
                    ch.step(ve.tensor_sub(tq[:], xs[:, cts(c, tb)], mp[:]))
                    ch.sync(ve)
                    ch.step(ve.tensor_mul(xn1b[:, cts(c, tb)], tq[:], rp[:]))
                ch.milestone(ve, S["s_app"])

    return nc


def prep_qkv(x1T_full, ins, core):
    i0, i1 = core * TPC, (core + 1) * TPC
    ce, se = rope_planes()
    return {
        "x1c": np.ascontiguousarray(x1T_full[:, i0:i1].astype(np.float32)),
        "wqt": np.ascontiguousarray(ins["L1_attn_wq"].T.astype(np.float32)),
        "wkat": np.ascontiguousarray(ins["L1_attn_wk_a"].T.astype(np.float32)),
        "wkbt": np.ascontiguousarray(ins["L1_attn_wk_b"].T.astype(np.float32)),
        "wvat": np.ascontiguousarray(ins["L1_attn_wv_a"].T.astype(np.float32)),
        "wvbt": np.ascontiguousarray(ins["L1_attn_wv_b"].T.astype(np.float32)),
        "cep": ce, "sep": se, "onf": np.ones((128, 128), np.float32),
    }



QB = 512
NQB = SEQ // QB


def build_attn():
    nc = bass.Bass("TRN2", target_bir_lowering=False)
    qr_d = nc.dram_tensor("qr_", [128, TOK], BF16, kind="ExternalInput").ap()
    kr_d = nc.dram_tensor("kr_", [128, TOK], BF16, kind="ExternalInput").ap()
    vt_d = nc.dram_tensor("vt_", [TOK, 128], BF16, kind="ExternalInput").ap()
    e2_d = nc.dram_tensor("e2", [128, 128], F32, kind="ExternalInput").ap()
    cst_d = nc.dram_tensor("cst", [128, 2], F32, kind="ExternalInput").ap()
    ao_out = nc.dram_tensor("ao_out", [128, TOK], BF16, kind="ExternalOutput").ap()

    NKC = SEQ // 128

    with ExitStack() as st:
        E = st.enter_context
        qr = E(nc.sbuf_tensor("e_qr", [128, TOK], BF16))
        kr = E(nc.sbuf_tensor("e_kr", [128, TOK], BF16))
        vv = E(nc.sbuf_tensor("e_vv", [128, NKC * 130], BF16))
        aob = E(nc.sbuf_tensor("e_aob", [128, TOK], BF16))
        P0 = E(nc.sbuf_tensor("e_P0", [128, 2 * QB], BF16))
        P1 = E(nc.sbuf_tensor("e_P1", [128, 2 * QB], BF16))
        recs = E(nc.sbuf_tensor("e_recs", [128, QB], F32))
        aop0 = E(nc.sbuf_tensor("e_aop0", [64, QB], BF16))
        aop1 = E(nc.sbuf_tensor("e_aop1", [64, QB], BF16))
        e2 = E(nc.sbuf_tensor("e_e2", [128, 128], F32))
        cst = E(nc.sbuf_tensor("e_cst", [128, 2], F32))
        ps0 = [E(nc.psum_tensor("ps0a", [128, QB], F32)), E(nc.psum_tensor("ps0b", [128, QB], F32))]
        ps1 = [E(nc.psum_tensor("ps1a", [128, QB], F32)), E(nc.psum_tensor("ps1b", [128, QB], F32))]
        psP0 = E(nc.psum_tensor("psP0", [128, QB], F32))
        psP1 = E(nc.psum_tensor("psP1", [128, QB], F32))
        psB = E(nc.psum_tensor("psB", [128, QB], F32))
        sems = {}
        for nm in ["s_ld", "s_vl", "s_sc", "s_ex", "s_pvk", "s_rc", "s_bc", "s_ae",
                   "s_dv", "s_ao", "s_vch", "s_ach"]:
            sems[nm] = E(nc.semaphore(nm))
        S = sems
        block = E(nc.Block())

        @block.sync
        def _(sy):
            sy.dma_start(out=qr[:], in_=qr_d[:]).then_inc(S["s_ld"], 16)
            sy.dma_start(out=kr[:], in_=kr_d[:]).then_inc(S["s_ld"], 16)
            sy.dma_start(out=e2[:], in_=e2_d[:]).then_inc(S["s_ld"], 16)
            sy.dma_start(out=cst[:], in_=cst_d[:]).then_inc(S["s_ld"], 16)
            sy.wait_ge(S["s_ld"], 64)
            for b in range(2):
                if b > 0:
                    sy.wait_ge(S["s_pvk"], NKC * NQB)
                sy.wait_ge(S["s_vl"], 1 + 32 * b)
                sy.dma_start(out=vv[:].rearrange("p (kc w) -> p kc w", w=130)[:, :, 0:64],
                             in_=vt_d[b * SEQ:(b + 1) * SEQ, 0:64].rearrange("(kc p) d -> p kc d", p=128)
                             ).then_inc(S["s_vl"], 16)
                sy.wait_ge(S["s_vl"], 1 + 32 * b + 16)
                sy.dma_start(out=vv[:].rearrange("p (kc w) -> p kc w", w=130)[:, :, 65:129],
                             in_=vt_d[b * SEQ:(b + 1) * SEQ, 64:128].rearrange("(kc p) d -> p kc d", p=128)
                             ).then_inc(S["s_vl"], 16)
            sy.wait_ge(S["s_dv"], 2 * NQB)
            sy.dma_start(out=ao_out[:], in_=aob[:]).then_inc(S["s_ao"], 16)
            sy.wait_ge(S["s_ao"], 16)

        @block.tensor
        def _(te):
            te.wait_ge(S["s_ld"], 64)
            for b in range(2):
                te.wait_ge(S["s_vl"], 1 + 32 * (b + 1))
                for qb in range(NQB):
                    bq = b * NQB + qb
                    qsl = slice(b * SEQ + qb * QB, b * SEQ + (qb + 1) * QB)
                    for kc in range(NKC):
                        gk = bq * NKC + kc
                        if kc >= 2:
                            te.wait_ge(S["s_ex"], bq * NKC + kc - 1)
                        ksl = slice(b * SEQ + kc * 128, b * SEQ + (kc + 1) * 128)
                        te.matmul(ps0[kc % 2][:], kr[0:64, ksl], qr[0:64, qsl],
                                  start=True, stop=True, tile_position=(0, 0))
                        te.matmul(ps1[kc % 2][:], kr[64:128, ksl], qr[64:128, qsl],
                                  start=True, stop=True, tile_position=(64, 0)).then_inc(S["s_sc"], 1)
                        if kc > 0:
                            pk = kc - 1
                            te.wait_ge(S["s_ex"], bq * NKC + pk + 1)
                            if pk == 0:
                                te.wait_ge(S["s_ae"], 2 * bq)
                                te.wait_ge(S["s_rc"], bq)
                            te.matmul(psP0[0:65, :], vv[:, pk * 130:pk * 130 + 65],
                                      P0[:, (pk % 2) * QB:(pk % 2 + 1) * QB],
                                      start=(pk == 0), stop=False, skip_group_check=True)
                            te.matmul(psP1[0:65, :], vv[:, pk * 130 + 65:(pk + 1) * 130],
                                      P1[:, (pk % 2) * QB:(pk % 2 + 1) * QB],
                                      start=(pk == 0), stop=False, skip_group_check=True).then_inc(S["s_pvk"], 1)
                    pk = NKC - 1
                    te.wait_ge(S["s_ex"], bq * NKC + pk + 1)
                    te.matmul(psP0[0:65, :], vv[:, pk * 130:pk * 130 + 65],
                              P0[:, (pk % 2) * QB:(pk % 2 + 1) * QB],
                              start=False, stop=True, skip_group_check=True)
                    te.matmul(psP1[0:65, :], vv[:, pk * 130 + 65:(pk + 1) * 130],
                              P1[:, (pk % 2) * QB:(pk % 2 + 1) * QB],
                              start=False, stop=True, skip_group_check=True).then_inc(S["s_pvk"], 1)
                    te.wait_ge(S["s_rc"], bq + 1)
                    if bq > 0:
                        te.wait_ge(S["s_dv"], bq)
                    te.matmul(psB[:], e2[:], recs[:], start=True, stop=True).then_inc(S["s_bc"], 1)

        @block.scalar
        def _(sc):
            sc.wait_ge(S["s_ld"], 64)
            for b in range(2):
                for qb in range(NQB):
                    bq = b * NQB + qb
                    for kc in range(NKC):
                        sc.wait_ge(S["s_sc"], bq * NKC + kc + 1)
                        if kc >= 2:
                            sc.wait_ge(S["s_pvk"], bq * NKC + kc - 1)
                        sc.activation(P0[:, (kc % 2) * QB:(kc % 2 + 1) * QB], ps0[kc % 2][:],
                                      AF.Exp, bias=cst[:, 1:2], scale=cst[:, 0:1])
                        ins = sc.activation(P1[:, (kc % 2) * QB:(kc % 2 + 1) * QB], ps1[kc % 2][:],
                                            AF.Exp, bias=cst[:, 1:2], scale=cst[:, 0:1])
                        ins.then_inc(S["s_ex"], 1)
                    sc.wait_ge(S["s_pvk"], (bq + 1) * NKC)
                    if bq > 0:
                        sc.wait_ge(S["s_dv"], bq)
                    sc.activation(aop0[:], psP0[0:64, :], AF.Identity).then_inc(S["s_ae"], 1)
                    sc.activation(aop1[:], psP1[0:64, :], AF.Identity).then_inc(S["s_ae"], 1)

        @block.vector
        def _(ve):
            ch = _Chain(S["s_vch"])
            ch.step(ve.memset(recs[:], 0.0))
            ch.step(ve.memset(vv[:], 1.0))
            ch.milestone(ve, S["s_vl"])
            for b in range(2):
                for qb in range(NQB):
                    bq = b * NQB + qb
                    qsl = slice(b * SEQ + qb * QB, b * SEQ + (qb + 1) * QB)
                    ve.wait_ge(S["s_pvk"], (bq + 1) * NKC)
                    if bq > 0:
                        ve.wait_ge(S["s_bc"], bq)
                    ch.sync(ve)
                    ch.step(ve.reciprocal(recs[0:1, :], psP0[64:65, :]))
                    ch.step(ve.reciprocal(recs[32:33, :], psP1[64:65, :]))
                    ch.milestone(ve, S["s_rc"])
                    ve.wait_ge(S["s_bc"], bq + 1)
                    ve.wait_ge(S["s_ae"], 2 * (bq + 1))
                    ch.sync(ve)
                    ch.step(ve.tensor_mul(aob[0:64, qsl], aop0[:], psB[0:64, :]))
                    ch.sync(ve)
                    ch.step(ve.tensor_mul(aob[64:128, qsl], aop1[:], psB[64:128, :]))
                    ch.milestone(ve, S["s_dv"])

    return nc


def prep_attn(qT_full, kT_full, vtm_full, core):
    e2 = np.zeros((128, 128), np.float32)
    e2[0, 0:64] = 1.0
    e2[32, 64:128] = 1.0
    return {
        "qr_": np.ascontiguousarray(qT_full[core * 128:(core + 1) * 128, :]),
        "kr_": np.ascontiguousarray(kT_full[core * 128:(core + 1) * 128, :]),
        "vt_": np.ascontiguousarray(vtm_full[:, core * 128:(core + 1) * 128]),
        "e2": e2,
        "cst": np.ascontiguousarray(np.tile(np.array([[0.125, -16.0]], np.float32), (128, 1))),
    }



def build_final():
    nc = bass.Bass("TRN2", target_bir_lowering=False)
    aoc_d = nc.dram_tensor("aoc", [D, TPC], BF16, kind="ExternalInput").ap()
    x1c_d = nc.dram_tensor("x1c", [D, TPC], F32, kind="ExternalInput").ap()
    xnh_d = nc.dram_tensor("xnh", [128, NCH * (TPC + 1)], BF16, kind="ExternalInput").ap()
    wot_d = nc.dram_tensor("wot", [D, D], F32, kind="ExternalInput").ap()
    wrt_d = nc.dram_tensor("wrt", [D, D], F32, kind="ExternalInput").ap()
    wkt_d = nc.dram_tensor("wkt", [D, D], F32, kind="ExternalInput").ap()
    wvt_d = nc.dram_tensor("wvt", [D, D], F32, kind="ExternalInput").ap()
    mur_d = nc.dram_tensor("mur", [128, NCH], F32, kind="ExternalInput").ap()
    muk_d = nc.dram_tensor("muk", [128, NCH], F32, kind="ExternalInput").ap()
    x2_out = nc.dram_tensor("x2_out", [D, TPC], F32, kind="ExternalOutput").ap()

    with ExitStack() as st:
        E = st.enter_context
        aoc = E(nc.sbuf_tensor("f_aoc", [128, NCH * TPC], BF16))
        xnh = E(nc.sbuf_tensor("f_xnh", [128, NCH * (TPC + 1)], BF16))
        wot = E(nc.sbuf_tensor("f_wot", [128, NCH * D], BF16))
        wrt = E(nc.sbuf_tensor("f_wrt", [128, NCH * D], BF16))
        wkt = E(nc.sbuf_tensor("f_wkt", [128, NCH * D], BF16))
        wvt = E(nc.sbuf_tensor("f_wvt", [128, NCH * D], BF16))
        mur = E(nc.sbuf_tensor("f_mur", [128, NCH], F32))
        muk = E(nc.sbuf_tensor("f_muk", [128, NCH], F32))
        x1b = E(nc.sbuf_tensor("f_x1b", [128, NCH * TB], F32))
        xrb = E(nc.sbuf_tensor("f_xrb", [128, NCH * TB], BF16))
        xkb = E(nc.sbuf_tensor("f_xkb", [128, NCH * TB], BF16))
        rb = E(nc.sbuf_tensor("f_rb", [128, NCH * TB], BF16))
        kab = E(nc.sbuf_tensor("f_kab", [128, NCH * TB], BF16))
        tq = E(nc.sbuf_tensor("f_tq", [128, TB], F32))
        tq2 = E(nc.sbuf_tensor("f_tq2", [128, TB], F32))
        f1 = E(nc.sbuf_tensor("f_f1", [128, TB], F32))
        g1 = E(nc.sbuf_tensor("f_g1", [128, TB], F32))
        x2s = E(nc.sbuf_tensor("f_x2s", [128, NCH * TB], F32))
        psR = E(nc.psum_tensor("psR", [128, 512], F32))
        psK = E(nc.psum_tensor("psK", [128, 512], F32))
        psV = E(nc.psum_tensor("psV", [128, 512], F32))
        psW = E(nc.psum_tensor("psW", [128, 512], F32))
        sems = {}
        for nm in ["s_ld", "s_wg", "s_x1l", "s_xrk", "s_rk", "s_sig", "s_ka",
                   "s_ov", "s_f1", "s_sil", "s_epi", "s_x2", "s_vch", "s_ach"]:
            sems[nm] = E(nc.semaphore(nm))
        S = sems
        block = E(nc.Block())

        @block.sync
        def _(sy):
            sy.dma_start(out=aoc[:].rearrange("p (c t) -> p c t", c=NCH),
                         in_=aoc_d.rearrange("(c p) t -> p c t", p=128)).then_inc(S["s_ld"], 16)
            sy.dma_start(out=xnh[:], in_=xnh_d[:]).then_inc(S["s_ld"], 16)
            sy.dma_start(out=mur[:], in_=mur_d[:]).then_inc(S["s_ld"], 16)
            sy.dma_start(out=muk[:], in_=muk_d[:]).then_inc(S["s_ld"], 16)
            sy.wait_ge(S["s_ld"], 64)
            for tb in range(NTB):
                if tb >= 1:
                    sy.wait_ge(S["s_epi"], 8 * tb)
                sy.dma_start(out=x1b[:].rearrange("p (c t) -> p c t", c=NCH),
                             in_=x1c_d.rearrange("(c p) t -> p c t", p=128)[:, :, tb * TB:(tb + 1) * TB]
                             ).then_inc(S["s_x1l"], 16)
                sy.wait_ge(S["s_x1l"], 16 * (tb + 1))
                sy.wait_ge(S["s_epi"], 8 * (tb + 1))
                sy.dma_start(out=x2_out.rearrange("(c p) t -> p c t", p=128)[:, :, tb * TB:(tb + 1) * TB],
                             in_=x2s[:].rearrange("p (c t) -> p c t", c=NCH)).then_inc(S["s_x2"], 16)
                sy.wait_ge(S["s_x2"], 16 * (tb + 1))

        @block.gpsimd
        def _(gp):
            for w_sb, w_d in ((wot, wot_d), (wrt, wrt_d), (wkt, wkt_d), (wvt, wvt_d)):
                gp.dma_start(out=w_sb[:].rearrange("p (c o) -> p c o", c=NCH),
                             in_=w_d.rearrange("(c p) o -> p c o", p=128)).then_inc(S["s_wg"], 16)

        @block.tensor
        def _(te):
            te.wait_ge(S["s_ld"], 64)
            te.wait_ge(S["s_wg"], 64)
            for tb in range(NTB):
                te.wait_ge(S["s_xrk"], tb + 1)
                for d1 in range(NCH):
                    gg = tb * NCH + d1
                    if gg > 0:
                        te.wait_ge(S["s_sig"], gg)
                        te.wait_ge(S["s_ka"], gg)
                    for c in range(NCH):
                        te.matmul(psR[:], wrt[:, c * D + d1 * 128:c * D + (d1 + 1) * 128],
                                  xrb[:, c * TB:(c + 1) * TB], start=(c == 0), stop=(c == NCH - 1))
                    for c in range(NCH):
                        ins = te.matmul(psK[:], wkt[:, c * D + d1 * 128:c * D + (d1 + 1) * 128],
                                        xkb[:, c * TB:(c + 1) * TB], start=(c == 0), stop=(c == NCH - 1))
                    ins.then_inc(S["s_rk"], 1)
                te.wait_ge(S["s_sig"], 8 * (tb + 1))
                te.wait_ge(S["s_ka"], 8 * (tb + 1))
                for d2 in range(NCH):
                    gg = tb * NCH + d2
                    if gg > 0:
                        te.wait_ge(S["s_epi"], gg)
                    for c in range(NCH):
                        te.matmul(psV[:], wvt[:, c * D + d2 * 128:c * D + (d2 + 1) * 128],
                                  kab[:, c * TB:(c + 1) * TB], start=(c == 0), stop=(c == NCH - 1))
                    for c in range(NCH):
                        ins = te.matmul(psW[:], wot[:, c * D + d2 * 128:c * D + (d2 + 1) * 128],
                                        aoc[:, c * TPC + tb * TB:c * TPC + (tb + 1) * TB],
                                        start=(c == 0), stop=(c == NCH - 1))
                    ins.then_inc(S["s_ov"], 1)

        @block.scalar
        def _(sc):
            sc.wait_ge(S["s_ld"], 64)
            for tb in range(NTB):
                for d1 in range(NCH):
                    gg = tb * NCH + d1
                    sc.wait_ge(S["s_rk"], gg + 1)
                    sc.activation(rb[:, d1 * TB:(d1 + 1) * TB], psR[:],
                                  AF.Sigmoid).then_inc(S["s_sig"], 1)
                for d2 in range(NCH):
                    gg = tb * NCH + d2
                    sc.wait_ge(S["s_f1"], gg + 1)
                    sc.activation(g1[:], f1[:], AF.Silu).then_inc(S["s_sil"], 1)

        @block.vector
        def _(ve):
            ch = _Chain(S["s_vch"])
            ve.wait_ge(S["s_ld"], 64)
            for tb in range(NTB):
                if tb > 0:
                    ve.wait_ge(S["s_rk"], 8 * tb)
                for c in range(NCH):
                    cur = xnh[:, c * (TPC + 1) + tb * TB + 1:c * (TPC + 1) + (tb + 1) * TB + 1]
                    prv = xnh[:, c * (TPC + 1) + tb * TB:c * (TPC + 1) + (tb + 1) * TB]
                    ch.sync(ve)
                    ch.step(ve.tensor_sub(tq[:], cur, prv))
                    ch.sync(ve)
                    ch.step(ve.scalar_tensor_tensor(xrb[:, c * TB:(c + 1) * TB], tq[:], mur[:, c:c + 1],
                                                    prv, ALU.mult, ALU.add))
                    ch.step(ve.scalar_tensor_tensor(xkb[:, c * TB:(c + 1) * TB], tq[:], muk[:, c:c + 1],
                                                    prv, ALU.mult, ALU.add))
                ch.milestone(ve, S["s_xrk"])
                for d1 in range(NCH):
                    gg = tb * NCH + d1
                    ve.wait_ge(S["s_rk"], gg + 1)
                    ch.sync(ve)
                    ch.step(ve.tensor_scalar_max(tq2[:], psK[:], 0.0))
                    ch.sync(ve)
                    ch.step(ve.tensor_mul(kab[:, d1 * TB:(d1 + 1) * TB], tq2[:], tq2[:]))
                    ch.milestone(ve, S["s_ka"])
                for d2 in range(NCH):
                    gg = tb * NCH + d2
                    ve.wait_ge(S["s_ov"], gg + 1)
                    ve.wait_ge(S["s_x1l"], 16 * (tb + 1))
                    if tb > 0 and d2 == 0:
                        ve.wait_ge(S["s_x2"], 16 * tb)
                    ch.sync(ve)
                    ch.step(ve.tensor_mul(f1[:], rb[:, d2 * TB:(d2 + 1) * TB], psV[:]))
                    ch.milestone(ve, S["s_f1"])
                    ve.wait_ge(S["s_sil"], gg + 1)
                    ch.sync(ve)
                    ch.step(ve.tensor_mul(tq[:], g1[:], psW[:]))
                    ch.sync(ve)
                    ch.step(ve.tensor_add(x2s[:, d2 * TB:(d2 + 1) * TB],
                                          x1b[:, d2 * TB:(d2 + 1) * TB], tq[:]))
                    ch.milestone(ve, S["s_epi"])

    return nc


def prep_final(aoT_full, x1T_full, xn1T_full, ins, core):
    i0, i1 = core * TPC, (core + 1) * TPC
    xnh = np.zeros((128, NCH * (TPC + 1)), dtype=BF)
    for c in range(NCH):
        blockv = xn1T_full[c * 128:(c + 1) * 128, i0:i1]
        xnh[:, c * (TPC + 1) + 1:(c + 1) * (TPC + 1)] = blockv
        if core % 4 != 0:
            xnh[:, c * (TPC + 1)] = xn1T_full[c * 128:(c + 1) * 128, i0 - 1]
    return {
        "aoc": np.ascontiguousarray(aoT_full[:, i0:i1]),
        "x1c": np.ascontiguousarray(x1T_full[:, i0:i1].astype(np.float32)),
        "xnh": xnh,
        "wot": np.ascontiguousarray(ins["L1_attn_wo"].T.astype(np.float32)),
        "wrt": np.ascontiguousarray(ins["L1_ffn_W_r"].T.astype(np.float32)),
        "wkt": np.ascontiguousarray(ins["L1_ffn_W_k"].T.astype(np.float32)),
        "wvt": np.ascontiguousarray(ins["L1_ffn_W_v"].T.astype(np.float32)),
        "mur": _chunkvec(np.asarray(ins["L1_ffn_mu_r"])),
        "muk": _chunkvec(np.asarray(ins["L1_ffn_mu_k"])),
    }


def _run(nc, maps):
    return run_bass_kernel_spmd(nc, maps, core_ids=list(range(NC)))


def kernel(**inputs):
    x = np.asarray(inputs["L0_attn_x"], dtype=np.float32)
    xT_full = np.ascontiguousarray(x.reshape(TOK, D).T)

    res = _run(build_ln0(), prep_ln0(x))
    xnT = np.concatenate([res.results[i]["xn_out"] for i in range(NC)], axis=1)

    res = _run(build_conv(), [prep_conv(xnT, np.asarray(inputs["L0_attn_t"]), i) for i in range(NC)])
    sT = np.concatenate([res.results[i]["s_out"] for i in range(NC)], axis=0)

    res = _run(build_ffn0(), [prep_ffn0(xnT, sT, xT_full, inputs, i) for i in range(NC)])
    x1T = np.concatenate([res.results[i]["x1_out"] for i in range(NC)], axis=1)

    res = _run(build_qkv(), [prep_qkv(x1T, inputs, i) for i in range(NC)])
    xn1T = np.concatenate([res.results[i]["xn1_out"] for i in range(NC)], axis=1)
    qT = np.concatenate([res.results[i]["q_out"] for i in range(NC)], axis=1)
    kT = np.concatenate([res.results[i]["k_out"] for i in range(NC)], axis=1)
    vtm = np.concatenate([res.results[i]["v_out"] for i in range(NC)], axis=0)

    res = _run(build_attn(), [prep_attn(qT, kT, vtm, i) for i in range(NC)])
    aoT = np.concatenate([res.results[i]["ao_out"] for i in range(NC)], axis=0)

    res = _run(build_final(), [prep_final(aoT, x1T, xn1T, inputs, i) for i in range(NC)])
    x2T = np.concatenate([res.results[i]["x2_out"] for i in range(NC)], axis=1)

    return np.ascontiguousarray(x2T.T).reshape(2, SEQ, D)
